# version 01 (final)
# baseline (speedup 1.0000x reference)
import contextlib
import numpy as np
import concourse.bass as bass
import concourse.mybir as mybir
from concourse.bass_utils import run_bass_kernel_spmd

F32 = mybir.dt.float32
BF16 = mybir.dt.bfloat16
AF = mybir.ActivationFunctionType
ALU = mybir.AluOpType

D = 1024
S = 8192
T = 32768
TSH = 4096
TT = 512
NT = 64
NTS = 8
CH = 32
NBLK = 256
NB = 2
E = 8
EPS1 = 1e-5
ENG_ORDER = ("sy", "gp", "se", "ve", "te")


MERGE_VE = {"cid", "con", "cwr", "cbu", "ce1", "wcst", "xcast", "var", "inva", "qrow",
            "prod", "rE", "Brow", "x2t1", "x2t2", "x2ts", "evv", "xc2", "xnt1", "xnt2",
            "xnc", "sel", "stev", "rv1", "rm1", "rl2", "rv2", "rm2", "rw2", "rcw1",
            "rcw2", "acz", "w1c", "w2c", "yev", "acc", "zc", "rz", "Bz", "fz1", "fz2",
            "fz3", "fo1", "fo2", "r0", "B0", "htz"}
MERGE_SE = {"sq", "mrow", "msq", "sqa", "asum", "sq2", "evg", "evh", "hmz", "sqz",
            "sqz2", "sil", "h1ev", "cwbe", "cwev", "rw1"}
MERGE_TE = {"mmS1", "mmS2", "mmZ1", "mmZ2", "uvmm", "pjmm", "rmm", "rtr", "bcA", "bcB",
            "bcAz", "bcBz", "h1mmz", "ymmz", "cwbc"}


def merge_label(lbl):
    if lbl in MERGE_VE:
        return "vem"
    if lbl in MERGE_SE:
        return "sem2"
    if lbl in MERGE_TE:
        return "tem"
    return lbl


class Op:
    __slots__ = ("eng", "emit", "deps", "label", "dma", "ordinal", "signaled", "sem")

    def __init__(self, eng, emit, deps, label, dma):
        self.eng, self.emit, self.deps, self.label, self.dma = eng, emit, deps, label, dma
        self.ordinal = 0
        self.signaled = False
        self.sem = None


class Prog:
    def __init__(self, nc):
        self.nc = nc
        self.ops = []
        self.counts = {}

    def op(self, eng, emit, deps=(), label=None, dma=False):
        o = Op(eng, emit, [d for d in deps if d is not None and not (isinstance(d, tuple) and d[1] is None)],
               merge_label(label) if label else ("q_" + eng), dma)
        o.ordinal = self.counts.get(o.label, 0)
        self.counts[o.label] = o.ordinal + 1
        self.ops.append(o)
        return o

    def dma(self, eng, out, in_, deps=(), label=None):
        return self.op(eng, lambda e, o=out, i=in_: e.dma_start(out=o, in_=i),
                       deps=deps, label=label or ("dma_" + eng), dma=True)

    def finalize(self, block, es):
        for o in self.ops:
            if o.dma:
                o.signaled = True
        for o in self.ops:
            for d in o.deps:
                kind, dd = ("raw", d) if isinstance(d, Op) else d
                if kind == "war" and dd.eng == o.eng:
                    continue
                dd.signaled = True
        sig_labels = {o.label for o in self.ops if o.signaled}
        for o in self.ops:
            if o.label in sig_labels:
                o.signaled = True
        sems = {}
        for o in self.ops:
            if o.signaled and o.label not in sems:
                sems[o.label] = es.enter_context(self.nc.semaphore(("s_" + o.label)[:28]))
            o.sem = sems.get(o.label)
        assert len(sems) <= 250, len(sems)
        for lbl in sig_labels:
            tot = sum((16 if o.dma else 1) for o in self.ops if o.label == lbl)
            assert tot < 32000, (lbl, tot)

        def mk(engname):
            def run(eng):
                seen = {}
                for o in self.ops:
                    if o.eng != engname:
                        continue
                    for d in o.deps:
                        kind, dd = ("raw", d) if isinstance(d, Op) else d
                        if kind == "war" and dd.eng == o.eng:
                            continue
                        step = 16 if dd.dma else 1
                        val = (dd.ordinal + 1) * step
                        if seen.get(dd.label, -1) >= val:
                            continue
                        eng.wait_ge(dd.sem, val)
                        seen[dd.label] = val
                    ins = o.emit(eng)
                    if o.signaled:
                        ins.then_inc(o.sem, 16 if o.dma else 1)
            return run

        decs = {"sy": block.sync, "gp": block.gpsimd, "se": block.scalar,
                "ve": block.vector, "te": block.tensor}
        for name in ENG_ORDER:
            decs[name](mk(name))


def build(upto=3, debug_scratch=False, nt_p1=NT):
    p1cut = 9
    p0cut = 9
    nc = bass.Bass("TRN2", target_bir_lowering=False)

    xT = nc.dram_tensor("xT", [128, 8, T], F32, kind="ExternalInput").ap()
    xTs = nc.dram_tensor("xTs", [128, 8, TSH], F32, kind="ExternalInput").ap()
    w_in = nc.dram_tensor("w_in", [128, 16384], F32, kind="ExternalInput").ap()
    w_h = nc.dram_tensor("w_h", [128, 8192], F32, kind="ExternalInput").ap()
    w_out = nc.dram_tensor("w_out", [128, 8192], F32, kind="ExternalInput").ap()
    w1b = nc.dram_tensor("w1b", [E, 32, 128, 8, 128], F32, kind="ExternalInput").ap()
    w2b = nc.dram_tensor("w2b", [E, 8, 128, 32, 128], F32, kind="ExternalInput").ap()
    wrb = nc.dram_tensor("wrb", [128, 64], F32, kind="ExternalInput").ap()
    bu_c = nc.dram_tensor("bu_c", [128, 8], F32, kind="ExternalInput").ap()
    bv_c = nc.dram_tensor("bv_c", [128, 8], F32, kind="ExternalInput").ap()
    bo_c = nc.dram_tensor("bo_c", [128, 8], F32, kind="ExternalInput").ap()
    b1_c = nc.dram_tensor("b1_c", [128, E, 32], F32, kind="ExternalInput").ap()
    b2_c = nc.dram_tensor("b2_c", [128, E, 8], F32, kind="ExternalInput").ap()
    lnw_c = nc.dram_tensor("lnw_c", [128, 8], F32, kind="ExternalInput").ap()
    lnb_c = nc.dram_tensor("lnb_c", [128, 8], F32, kind="ExternalInput").ap()
    lw2_c = nc.dram_tensor("lw2_c", [128, 8], F32, kind="ExternalInput").ap()
    lb2_c = nc.dram_tensor("lb2_c", [128, 8], F32, kind="ExternalInput").ap()
    br_r = nc.dram_tensor("br_r", [16, 8], F32, kind="ExternalInput").ap()
    id128 = nc.dram_tensor("id128", [128, 128], F32, kind="ExternalInput").ap()
    ones_c = nc.dram_tensor("ones_c", [128, 1], F32, kind="ExternalInput").ap()
    ones_r = nc.dram_tensor("ones_r", [1, 128], F32, kind="ExternalInput").ap()
    flags = nc.dram_tensor("flags", [128, 8], F32, kind="ExternalInput").ap()
    outT = nc.dram_tensor("outT", [NTS, 128, 8, TT], F32, kind="ExternalOutput").ap()

    dbg = "ExternalOutput" if debug_scratch else "Internal"
    v_scr = nc.dram_tensor("v_scr", [NBLK, 128, 8, CH, 4], BF16, kind=dbg).ap()
    g_scr = nc.dram_tensor("g_scr", [NBLK, 128, 8, CH, 4], BF16, kind=dbg).ap()
    gh_scr = nc.dram_tensor("gh_scr", [NBLK, 128, 8, CH, 4], BF16, kind=dbg).ap()
    hs_scr = nc.dram_tensor("hs_scr", [NBLK, 128, 8, CH, 4], BF16,
                            kind="ExternalOutput" if (debug_scratch or upto == 2) else "Internal").ap()
    st_scr = nc.dram_tensor("st_scr", [NTS, 128, 8, TT], BF16, kind=dbg).ap()
    xn_scr = nc.dram_tensor("xn_scr", [NTS, 128, 8, TT], BF16, kind=dbg).ap()
    cw_scr = nc.dram_tensor("cw_scr", [NTS, 16, TT], F32, kind=dbg).ap()
    acc_dbg = nc.dram_tensor("acc_dbg", [2, 128, 8, 2048], BF16, kind=dbg).ap() if debug_scratch else None
    h1_dbg = nc.dram_tensor("h1_dbg", [128, 32, TT], BF16, kind=dbg).ap() if debug_scratch else None
    cwb_dbg = nc.dram_tensor("cwb_dbg", [128, TT], F32, kind=dbg).ap() if debug_scratch else None

    es = contextlib.ExitStack()

    def sbt(name, shape, dt):
        return es.enter_context(nc.sbuf_tensor(name, shape, dt))

    ABF = sbt("ABF", [128, 53248], BF16)
    AFF = sbt("AFF", [128, 12288], F32)

    def vb(off, *shape):
        n = int(np.prod(shape))
        if off + n > ABF.shape[1]:
            off = 0
        a = ABF[:, off:off + n]
        if len(shape) <= 1:
            return a
        dims = "abcd"[:len(shape)]
        return a.rearrange(f"p ({' '.join(dims)}) -> p {' '.join(dims)}",
                           **{k: v for k, v in zip(dims, shape)})

    def vf(off, *shape):
        n = int(np.prod(shape))
        a = AFF[:, off:off + n]
        if len(shape) <= 1:
            return a
        dims = "abcd"[:len(shape)]
        return a.rearrange(f"p ({' '.join(dims)}) -> p {' '.join(dims)}",
                           **{k: v for k, v in zip(dims, shape)})

    wi_bf = vb(0, 8, 16, 128)
    h1big = vb(0, 32, TT)
    wh_bf = vb(16384, 8, 8, 128)
    wo_bf = vb(24576, 8, 8, 128)
    accb = vb(16384, 8, 2048)
    xb_ = vb(32768, 8, TT)
    sqb = vb(36864, 8, TT)
    xn2a = vb(40960, 2, 8, TT)
    stgg = vb(49152, 2, TT)
    stgh = vb(50176, 2, TT)
    stgv = vb(51200, 2, TT)
    vblk = vb(32768, NB, 8, CH, 4)
    gblk = vb(34816, NB, 8, CH, 4)
    ghblk = vb(36864, NB, 8, CH, 4)
    hhb = vb(38912, 8, CH, 4)
    candb = vb(40960, 8, TT)
    def _abf(a, b):
        return ABF[:, a:b] if b <= ABF.shape[1] else ABF[:, 0:b - a]
    candbf = _abf(40960, 45056)
    hselA = _abf(45056, 49152)
    hselB = _abf(49152, 53248)
    hselA3 = vb(45056, 8, TT)
    hselB3 = vb(49152, 8, TT)
    xnb3 = vb(32768, 8, TT)
    xnb3 = vb(36864, 8, TT)
    stbf = vb(40960, 8, TT)
    stbf = vb(32768, 8, TT)
    w1bb = vb(32768, 2, 8, 128)
    w2bb = vb(34816, 2, 32, 128)
    xnb3m = vb(43008, 8, TT)
    zb = vb(32768, 8, TT)
    sqzb = vb(36864, 8, TT)
    stbf2 = vb(43008, 8, TT)
    xnbf2 = vb(47104, 8, TT)

    xf0 = vf(0, 8, TT)
    xf1 = vf(4096, 8, TT)
    xnf = vf(4096, 8, TT)
    xsf = vf(0, 8, TT)
    o1f = vf(4096, 8, TT)
    w2st0 = vf(0, 32, 128)
    w2st1 = vf(4096, 32, 128)
    w1st0 = vf(8192, 8, 128)
    w1st1 = vf(9216, 8, 128)
    t1f = vf(8192, TT)
    t2f = vf(8704, TT)
    tg1 = vf(8192, TT)
    tg2 = vf(8704, TT)
    cwbf = vf(10240, TT)
    gsil = vf(10240, TT)
    yevf = vf(10752, TT)
    cwt8f = vf(11264, TT)

    rowa = sbt("rowa", [1, 8, TT], F32)
    rowb = sbt("rowb", [1, 4, TT], F32)
    idf = sbt("idf", [128, 128], F32)
    idb = sbt("idb", [128, 128], BF16)
    onc_f = sbt("onc_f", [128, 1], F32)
    onc_b = sbt("onc_b", [128, 1], BF16)
    onr_f = sbt("onr_f", [1, 128], F32)
    brr_f = sbt("brr_f", [16, 8], F32)
    eps1c = sbt("eps1c", [1, 1], F32)
    flg = sbt("flg", [128, 8], F32)
    bu = sbt("bu", [128, 8], F32)
    bun = sbt("bun", [128, 8], F32)
    bv = sbt("bv", [128, 8], F32)
    bo = sbt("bo", [128, 8], F32)
    b1c = sbt("b1c", [128, E, 32], F32)
    b2c = sbt("b2c", [128, E, 8], F32)
    lnw = sbt("lnw", [128, 8], F32)
    lnb = sbt("lnb", [128, 8], F32)
    lw2 = sbt("lw2", [128, 8], F32)
    lb2 = sbt("lb2", [128, 8], F32)
    wr_f = sbt("wr_f", [128, 64], F32)
    wr_b = sbt("wr_b", [128, 64], BF16)
    hT = sbt("hT", [128, 32], F32)
    c_t = sbt("c_t", [128, 32], F32)
    p_t = sbt("p_t", [128, 32], F32)
    q_t = sbt("q_t", [128, 32], F32)
    cwsb = sbt("cwsb", [128, 8], F32)
    l2sb = sbt("l2sb", [128, 8], F32)
    m1sb = sbt("m1sb", [128, 8], F32)
    m2sb = sbt("m2sb", [128, 8], F32)
    cwgs = sbt("cwgs", [128, 8], F32)
    v1sb = sbt("v1sb", [128, 1], F32)
    v2sb = sbt("v2sb", [128, 1], F32)
    w1w = sbt("w1w", [128, 1], F32)
    w2w = sbt("w2w", [128, 1], F32)
    cwrow = sbt("cwrow", [1, TT], F32)

    pb = [es.enter_context(nc.psum_tensor(f"pb{i}", [128, TT], F32)) for i in range(8)]
    pbuse = [None] * 8

    P = Prog(nc)

    smalls = []
    for dst, src in [(idf[:], id128[:, :]), (onc_f[:], ones_c[:, :]), (onr_f[:], ones_r[:, :]),
                     (brr_f[:, :], br_r[:, :]), (flg[:], flags[:, :]), (bu[:], bu_c[:, :]),
                     (bv[:], bv_c[:, :]), (bo[:], bo_c[:, :]), (lnw[:], lnw_c[:, :]),
                     (lnb[:], lnb_c[:, :]), (lw2[:], lw2_c[:, :]), (lb2[:], lb2_c[:, :]),
                     (b1c[:], b1_c[:, :, :]), (b2c[:], b2_c[:, :, :]), (wr_f[:], wrb[:, :])]:
        smalls.append(P.dma("sy", dst, src, label="ldsm"))
    c_id = P.op("ve", lambda e: e.tensor_copy(idb[:], idf[:]), deps=[smalls[-1]], label="cid")
    c_on = P.op("ve", lambda e: e.tensor_copy(onc_b[:], onc_f[:]), deps=[smalls[-1]], label="con")
    c_wr = P.op("ve", lambda e: e.tensor_copy(wr_b[:], wr_f[:]), deps=[smalls[-1]], label="cwr")
    c_bu = P.op("ve", lambda e: e.tensor_scalar(bun[:], bu[:], -1.0, None, op0=ALU.mult),
                deps=[smalls[-1]], label="cbu")
    c_e1 = P.op("ve", lambda e: e.memset(eps1c[:], EPS1), label="ce1")

    wdsts = ([ABF[:, i * 4096:(i + 1) * 4096] for i in range(4)]
             + [ABF[:, 16384 + i * 4096:16384 + (i + 1) * 4096] for i in range(2)]
             + [ABF[:, 24576 + i * 4096:24576 + (i + 1) * 4096] for i in range(2)])
    wsrcs = ([w_in[:, i * 4096:(i + 1) * 4096] for i in range(4)]
             + [w_h[:, i * 4096:(i + 1) * 4096] for i in range(2)]
             + [w_out[:, i * 4096:(i + 1) * 4096] for i in range(2)])
    wcast = []
    for i in range(8 if p0cut >= 2 else 0):
        buf = xf0 if i % 2 == 0 else xf1
        deps = [("war", wcast[i - 2])] if i >= 2 else []
        bflat = AFF[:, 0:4096] if i % 2 == 0 else AFF[:, 4096:8192]
        ldw_ = P.dma("sy", bflat, wsrcs[i], deps=deps, label=f"ldwb{i % 2}")
        wcast.append(P.op("ve", lambda e, b=bflat, d=wdsts[i]: e.tensor_copy(d, b),
                          deps=[ldw_], label="wcst"))

    p1wr = [[] for _ in range(NT)]
    ptile = {}
    for t in range(nt_p1 if p1cut >= 1 else 0):
        xbuf = xf0 if t % 2 == 0 else xf1
        deps = [wcast[6], wcast[7]] if t < 2 else [("war", ptile[t - 2]["xlast"])]
        xl = P.dma("sy", xbuf[:, :, :], xT[:, :, t * TT:(t + 1) * TT], deps=deps, label=f"xld{t % 2}")
        cb = P.op("ve", lambda e, b=xbuf: e.tensor_copy(xb_.rearrange("p c d -> p (c d)"),
                                                        b.rearrange("p c d -> p (c d)")),
                  deps=[xl, ("war", ptile[t - 1]["mmS2_7"] if t else None)], label="xcast")
        sq = P.op("se", lambda e: e.activation(sqb.rearrange("p c d -> p (c d)"),
                                               xb_.rearrange("p c d -> p (c d)"), AF.Square),
                  deps=[cb, ("war", ptile[t - 1]["mmS2_7"] if t else None)], label="sq")
        mmS1 = []
        for dc in range(8):
            d0 = [cb, c_on, ("war", pbuse[4])] if dc == 0 else []
            mmS1.append(P.op("te", lambda e, dc=dc: e.matmul(pb[4][0:1, :], onc_b[:, :], xb_[:, dc, :],
                                                             start=(dc == 0), stop=(dc == 7)),
                             deps=d0, label="mmS1"))
        mmS2 = []
        for dc in range(8):
            d0 = [sq, c_on, ("war", pbuse[5])] if dc == 0 else []
            mmS2.append(P.op("te", lambda e, dc=dc: e.matmul(pb[5][0:1, :], onc_b[:, :], sqb[:, dc, :],
                                                             start=(dc == 0), stop=(dc == 7)),
                             deps=d0, label="mmS2"))
        if p1cut < 2:
            ptile[t] = dict(xlast=mmS2[7], mmS2_7=mmS2[7], uvlast=mmS2[7],
                            wrG=[None, None], wrH=[None, None], wrV=[None, None])
            p1wr[t] = []
            continue
        rowwar = ("war", ptile[t - 1]["xlast"] if t else None)
        mrow = P.op("se", lambda e: e.activation(rowa[0:1, 0, :], pb[4][0:1, :], AF.Identity,
                                                 scale=1.0 / D), deps=[mmS1[7], rowwar], label="mrow")
        msq = P.op("se", lambda e: e.activation(rowa[0:1, 1, :], pb[4][0:1, :], AF.Square,
                                                scale=1.0 / D), deps=[mmS1[7]], label="msq")
        pbuse[4] = msq
        var = P.op("ve", lambda e: e.scalar_tensor_tensor(rowa[0:1, 2, :], pb[5][0:1, :], 1.0 / D,
                                                          rowa[0:1, 1, :], op0=ALU.mult,
                                                          op1=ALU.subtract),
                   deps=[mmS2[7], msq, rowwar], label="var")
        pbuse[5] = var
        sqa = P.op("se", lambda e: e.activation(rowa[0:1, 3, :], rowa[0:1, 2, :], AF.Sqrt,
                                                bias=eps1c[0:1, 0:1]), deps=[var, c_e1], label="sqa")
        asum = P.op("se", lambda e: e.activation(rowa[0:1, 4, :], rowa[0:1, 2, :], AF.Identity,
                                                 bias=eps1c[0:1, 0:1]), deps=[var], label="asum")
        inva = P.op("ve", lambda e: e.reciprocal(rowa[0:1, 5, :], rowa[0:1, 4, :]),
                    deps=[asum], label="inva")
        q_ = P.op("ve", lambda e: e.tensor_tensor(rowa[0:1, 6, :], rowa[0:1, 2, :], rowa[0:1, 5, :],
                                                  op=ALU.mult), deps=[inva], label="qrow")
        sq2 = P.op("se", lambda e: e.activation(rowa[0:1, 7, :], rowa[0:1, 6, :], AF.Sqrt,
                                                bias=eps1c[0:1, 0:1]), deps=[q_], label="sq2")
        prod = P.op("ve", lambda e: e.tensor_tensor(rowb[0:1, 0, :], rowa[0:1, 3, :], rowa[0:1, 7, :],
                                                    op=ALU.mult), deps=[sqa, sq2], label="prod")
        rE = P.op("ve", lambda e: e.reciprocal(rowb[0:1, 1, :], rowb[0:1, 0, :]),
                  deps=[prod], label="rE")
        Brow = P.op("ve", lambda e: e.tensor_tensor(rowb[0:1, 2, :], rowa[0:1, 0, :], rowb[0:1, 1, :],
                                                    op=ALU.mult), deps=[rE, mrow], label="Brow")
        bcA = P.op("te", lambda e: e.matmul(pb[6][:, :], onr_f[:, :], rowb[0:1, 1, :],
                                            start=True, stop=True),
                   deps=[rE, smalls[-1], ("war", pbuse[6])], label="bcA")
        bcB = P.op("te", lambda e: e.matmul(pb[7][:, :], onr_f[:, :], rowb[0:1, 2, :],
                                            start=True, stop=True),
                   deps=[Brow, smalls[-1], ("war", pbuse[7])], label="bcB")
        x2par = t % 2
        x2last = None
        uvwar = ("war", ptile[t - 2]["uvlast"] if t >= 2 else None)
        for dc in range(8):
            tt1 = P.op("ve", lambda e, dc=dc, b=xbuf: e.tensor_tensor(t1f[:], b[:, dc, :], pb[6][:, :],
                                                                      op=ALU.mult),
                       deps=[bcA, uvwar] if dc == 0 else [], label="x2t1")
            tt2 = P.op("ve", lambda e: e.tensor_tensor(t2f[:], t1f[:], pb[7][:, :], op=ALU.subtract),
                       deps=[tt1] + ([bcB] if dc == 0 else []), label="x2t2")
            x2last = P.op("ve", lambda e, dc=dc, p2=x2par: e.tensor_scalar(
                xn2a[:, p2, dc, :], t2f[:], lnw[:, dc:dc + 1], lnb[:, dc:dc + 1],
                op0=ALU.mult, op1=ALU.add), deps=[tt2, smalls[-1], smalls[-1]], label="x2ts")
        pbuse[6] = x2last
        pbuse[7] = x2last
        evlast = {}
        uvlast = None
        if p1cut < 3:
            ptile[t] = dict(xlast=x2last, mmS2_7=mmS2[7], uvlast=x2last,
                            wrG=[None, None], wrH=[None, None], wrV=[None, None])
            p1wr[t] = []
            continue
        for ec in range(16):
            bnk = ec % 2
            d0 = [x2last, ("war", pbuse[bnk])] if True else []
            if ec > 0:
                d0 = [("war", pbuse[bnk])]
            if ec == 0:
                d0 = [x2last, wcast[3], ("war", pbuse[bnk])]
            mmu = []
            for dc in range(8):
                mmu.append(P.op("te", lambda e, ec=ec, dc=dc, p2=x2par, bnk=bnk: e.matmul(
                    pb[bnk][:, :], wi_bf[:, dc, ec, :], xn2a[:, p2, dc, :],
                    start=(dc == 0), stop=(dc == 7)), deps=d0 if dc == 0 else [], label="uvmm"))
            uvlast = mmu[7]
            spar = ec % 2
            if ec < 8:
                wg_prev = evlast.get(("g", spar)) or (ptile[t - 1]["wrG"][spar] if t else None)
                wh_prev = evlast.get(("h", spar)) or (ptile[t - 1]["wrH"][spar] if t else None)
                eg = P.op("se", lambda e, ec=ec, spar=spar, bnk=bnk: e.activation(
                    stgg[:, spar, :], pb[bnk][:, :], AF.Sigmoid, bias=bu[:, ec:ec + 1]),
                    deps=[mmu[7], smalls[-1], ("war", wg_prev)], label="evg")
                eh = P.op("se", lambda e, ec=ec, spar=spar, bnk=bnk: e.activation(
                    stgh[:, spar, :], pb[bnk][:, :], AF.Sigmoid, scale=-1.0, bias=bun[:, ec:ec + 1]),
                    deps=[mmu[7], c_bu, ("war", wh_prev)], label="evh")
                wg = P.dma("gp", g_scr[4 * t:4 * t + 4, :, ec, :, :].rearrange("k p s b -> p k s b"),
                           stgg[:, spar, :].rearrange("p (k sb) -> p k sb", k=4), deps=[eg], label=f"wrg{spar}")
                wh2 = P.dma("gp", gh_scr[4 * t:4 * t + 4, :, ec, :, :].rearrange("k p s b -> p k s b"),
                            stgh[:, spar, :].rearrange("p (k sb) -> p k sb", k=4), deps=[eh], label=f"wrh{spar}")
                p1wr[t] += [wg, wh2]
                evlast[("g", spar)] = wg
                evlast[("h", spar)] = wh2
                pbuse[bnk] = eh
            else:
                wv_prev = evlast.get(("v", spar)) or (ptile[t - 1]["wrV"][spar] if t else None)
                ev = P.op("ve", lambda e, ec=ec, spar=spar, bnk=bnk: e.tensor_scalar(
                    stgv[:, spar, :], pb[bnk][:, :], bv[:, ec - 8:ec - 7], None, op0=ALU.add),
                    deps=[mmu[7], smalls[-1], ("war", wv_prev)], label="evv")
                wv = P.dma("gp", v_scr[4 * t:4 * t + 4, :, ec - 8, :, :].rearrange("k p s b -> p k s b"),
                           stgv[:, spar, :].rearrange("p (k sb) -> p k sb", k=4), deps=[ev], label=f"wrv{spar}")
                p1wr[t].append(wv)
                evlast[("v", spar)] = wv
                pbuse[bnk] = ev
        ptile[t] = dict(xlast=x2last, mmS2_7=mmS2[7], uvlast=uvlast,
                        wrG=[evlast.get(("g", 0)), evlast.get(("g", 1))],
                        wrH=[evlast.get(("h", 0)), evlast.get(("h", 1))],
                        wrV=[evlast.get(("v", 0)), evlast.get(("v", 1))])

    finals = []
    if upto == 1:
        finals = [w for tl in p1wr for w in tl]
        if not finals:
            finals = ([ptile[nt_p1 - 1]["xlast"], ptile[nt_p1 - 1]["mmS2_7"]]
                      if ptile else ([wcast[-1], c_id, c_on, c_wr, c_bu] if wcast
                                       else [c_id, c_on, c_wr, c_bu, c_e1]))
    assert upto == 1 or nt_p1 == NT

    hsd = {}
    if upto >= 2:
        hmz = P.op("se", lambda e: e.memzero(hhb.rearrange("p a b c -> p (a b c)")),
                   deps=[wcast[5], ("war", ptile[NT - 1]["uvlast"] if False else None)], label="hmz")
        htz = P.op("ve", lambda e: e.memset(hT[:], 0.0), label="htz")
        ldv = {}
        cons = {}
        allwr = [w for tl in p1wr for w in tl]

        def emit_ld(blk):
            bpar = blk % NB
            base = allwr if blk < NB else list(p1wr[blk // 4])
            wdep = [("war", cons[blk - NB])] if blk >= NB else []
            l1 = P.dma("sy", vblk[:, bpar, :, :, :], v_scr[blk, :, :, :, :],
                       deps=base + wdep, label=f"ldv{bpar}")
            l2 = P.dma("sy", gblk[:, bpar, :, :, :], g_scr[blk, :, :, :, :],
                       deps=base + wdep, label=f"ldg{bpar}")
            l3 = P.dma("sy", ghblk[:, bpar, :, :, :], gh_scr[blk, :, :, :, :],
                       deps=base + wdep, label=f"ldgh{bpar}")
            ldv[blk] = (l1, l2, l3)

        for blk in range(NB):
            emit_ld(blk)
        prev = {}
        for s_ in range(S):
            bpar = (s_ // CH) % NB
            c = s_ % CH
            l1, l2, l3 = ldv[s_ // CH]
            sdeps = [l1, c_id]
            if s_ == 0:
                sdeps += [hmz, wcast[5], ("war", pbuse[0]), ("war", pbuse[1])]
            else:
                sdeps += [prev["cast"], ("war", prev["tanh"])]
            seed = P.op("te", lambda e, bpar=bpar, c=c: e.matmul(
                pb[0][:, 0:32].rearrange("p (k b) -> p k b", k=8), idb[:, :],
                vblk[:, bpar, :, c, :], start=True, stop=False, skip_group_check=True),
                deps=sdeps, label="seed")
            last = None
            for jc in range(8):
                for kc in range(8):
                    last = P.op("te", lambda e, jc=jc, kc=kc, s_=s_: e.matmul(
                        pb[0][:, jc * 4:(jc + 1) * 4], wh_bf[:, jc, kc, :],
                        hhb[:, kc, (s_ - 1) % CH, :], start=False, stop=(jc == 7 and kc == 7),
                        skip_group_check=True), deps=[], label=("smmz" if (jc == 7 and kc == 7) else "smm"))
            tanh = P.op("se", lambda e: e.activation(c_t[:, :], pb[0][:, 0:32], AF.Tanh),
                        deps=[last], label="tanh")
            pop = P.op("ve", lambda e, bpar=bpar, c=c: e.tensor_tensor(
                p_t.rearrange("p (k b) -> p k b", k=8), ghblk[:, bpar, :, c, :],
                hT.rearrange("p (k b) -> p k b", k=8), op=ALU.mult),
                deps=[l3, prev["h"] if s_ else htz], label="scp")
            qop = P.op("ve", lambda e, bpar=bpar, c=c: e.tensor_tensor(
                q_t.rearrange("p (k b) -> p k b", k=8), gblk[:, bpar, :, c, :],
                c_t.rearrange("p (k b) -> p k b", k=8), op=ALU.mult),
                deps=[tanh, l2], label="scq")
            hop = P.op("ve", lambda e: e.tensor_tensor(hT[:], p_t[:], q_t[:], op=ALU.add),
                       deps=[pop, qop], label="sch")
            castop = P.op("se", lambda e, c=c: e.copy(hhb[:, :, c, :],
                                                      hT.rearrange("p (k b) -> p k b", k=8)),
                          deps=[hop, ("war", hsd[s_ // CH - 1] if (c == 0 and s_ >= CH) else None)],
                          label="hcast")
            prev = dict(cast=castop, tanh=tanh, h=hop)
            if c == CH - 1:
                blk = s_ // CH
                hsd[blk] = P.dma("gp", hs_scr[blk, :, :, :, :], hhb[:, :, :, :], deps=[castop],
                                 label="hsd")
                cons[blk] = hop
                if blk + NB < NBLK:
                    emit_ld(blk + NB)
        pbuse[0] = prev["tanh"]
        if upto == 2:
            finals = [hsd[b] for b in range(NBLK)]

    if upto >= 3:
        lasthsd = hsd[NBLK - 1]
        stws = {}
        xnws = {}
        cwws = {}
        selv = {}
        pproj = {}
        for ts in range(NTS):
            sel = None
            for j in range(8):
                war_c = [("war", selv[(ts, j - 1)] if j else (selv[(ts - 1, 7)] if ts else None))]
                cdma = []
                for bp in range(4):
                    gblk_ = j * 32 + ts * 4 + bp
                    dd = war_c if (j or ts) else [lasthsd, ("war", ptile[NT - 1]["uvlast"])]
                    if j == 0 and ts == 0:
                        dd = [hsd[b] for b in range(NBLK)] + [("war", ptile[NT - 1]["uvlast"])]
                    cdma.append(P.dma("sy", candb[:, :, bp * 128:(bp + 1) * 128]
                                      .rearrange("p k (s b) -> p k s b", s=CH),
                                      hs_scr[gblk_, :, :, :, :], deps=dd, label="chs"))
                dstf = hselA if j % 2 == 0 else hselB
                srcf = hselB if j % 2 == 0 else hselA
                if j == 0:
                    sel = P.op("ve", lambda e, dstf=dstf: e.tensor_scalar(
                        dstf, candbf, flg[:, 0:1], None, op0=ALU.mult),
                        deps=cdma + [smalls[-1]], label="sel")
                else:
                    sel = P.op("ve", lambda e, j=j, dstf=dstf, srcf=srcf: e.scalar_tensor_tensor(
                        dstf, candbf, flg[:, j:j + 1], srcf, op0=ALU.mult, op1=ALU.add),
                        deps=cdma + [sel], label="sel")
                selv[(ts, j)] = sel
            hs_fin = hselB3
            prowwar = ("war", pproj[ts - 1]["stl"] if ts else None)
            xld = P.dma("sy", xsf[:, :, :], xTs[:, :, ts * TT:(ts + 1) * TT],
                        deps=[("war", ptile[NT - 1]["xlast"] if ts == 0 else pproj[ts - 1]["xnl"])],
                        label="xsld")
            cb2 = P.op("ve", lambda e: e.tensor_copy(xb_.rearrange("p c d -> p (c d)"),
                                                     xsf.rearrange("p c d -> p (c d)")),
                       deps=[xld, ("war", ptile[NT - 1]["mmS2_7"] if ts == 0 else pproj[ts - 1]["mmS2_7"]),
                             ("war", pproj[ts - 1]["stw"] if ts else None)], label="xc2")
            sq_2 = P.op("se", lambda e: e.activation(sqb.rearrange("p c d -> p (c d)"),
                                                     xb_.rearrange("p c d -> p (c d)"), AF.Square),
                        deps=[cb2, ("war", selv[(ts, 7)]),
                              ("war", pproj[ts - 1]["xnc"] if ts else None),
                              ("war", pproj[ts - 1]["rmmlast"] if ts else None)], label="sq")
            mmS1b = []
            for dc in range(8):
                d0 = [cb2, ("war", pbuse[4])] if dc == 0 else []
                mmS1b.append(P.op("te", lambda e, dc=dc: e.matmul(pb[4][0:1, :], onc_b[:, :],
                                                                  xb_[:, dc, :], start=(dc == 0),
                                                                  stop=(dc == 7)), deps=d0, label="mmS1"))
            mmS2b = []
            for dc in range(8):
                d0 = [sq_2, ("war", pbuse[5])] if dc == 0 else []
                mmS2b.append(P.op("te", lambda e, dc=dc: e.matmul(pb[5][0:1, :], onc_b[:, :],
                                                                  sqb[:, dc, :], start=(dc == 0),
                                                                  stop=(dc == 7)), deps=d0, label="mmS2"))
            mrow2 = P.op("se", lambda e: e.activation(rowa[0:1, 0, :], pb[4][0:1, :], AF.Identity,
                                                      scale=1.0 / D), deps=[mmS1b[7], prowwar], label="mrow")
            msq2 = P.op("se", lambda e: e.activation(rowa[0:1, 1, :], pb[4][0:1, :], AF.Square,
                                                     scale=1.0 / D), deps=[mmS1b[7]], label="msq")
            pbuse[4] = msq2
            var2 = P.op("ve", lambda e: e.scalar_tensor_tensor(rowa[0:1, 2, :], pb[5][0:1, :], 1.0 / D,
                                                               rowa[0:1, 1, :], op0=ALU.mult,
                                                               op1=ALU.subtract),
                        deps=[mmS2b[7], msq2, prowwar], label="var")
            pbuse[5] = var2
            sqa2 = P.op("se", lambda e: e.activation(rowa[0:1, 3, :], rowa[0:1, 2, :], AF.Sqrt,
                                                     bias=eps1c[0:1, 0:1]), deps=[var2], label="sqa")
            r0 = P.op("ve", lambda e: e.reciprocal(rowa[0:1, 5, :], rowa[0:1, 3, :]),
                      deps=[sqa2], label="r0")
            B0 = P.op("ve", lambda e: e.tensor_tensor(rowa[0:1, 6, :], rowa[0:1, 0, :], rowa[0:1, 5, :],
                                                      op=ALU.mult), deps=[r0, mrow2], label="B0")
            bcA0 = P.op("te", lambda e: e.matmul(pb[6][:, :], onr_f[:, :], rowa[0:1, 5, :],
                                                 start=True, stop=True),
                        deps=[r0, smalls[-1], ("war", pbuse[6])], label="bcA")
            bcB0 = P.op("te", lambda e: e.matmul(pb[7][:, :], onr_f[:, :], rowa[0:1, 6, :],
                                                 start=True, stop=True),
                        deps=[B0, smalls[-1], ("war", pbuse[7])], label="bcB")
            xnl = None
            for dc in range(8):
                ttx = P.op("ve", lambda e, dc=dc: e.tensor_tensor(xnf[:, dc, :], xsf[:, dc, :],
                                                                  pb[6][:, :], op=ALU.mult),
                           deps=[bcA0] if dc == 0 else [], label="xnt1")
                xnl = P.op("ve", lambda e, dc=dc: e.tensor_tensor(xnf[:, dc, :], xnf[:, dc, :],
                                                                  pb[7][:, :], op=ALU.subtract),
                           deps=[ttx] + ([bcB0] if dc == 0 else []), label="xnt2")
            pbuse[6] = xnl
            pbuse[7] = xnl
            xnc = P.op("ve", lambda e: e.tensor_copy(xnb3.rearrange("p c d -> p (c d)"),
                                                     xnf.rearrange("p c d -> p (c d)")),
                       deps=[xnl, ("war", mmS2b[7])], label="xnc")
            xnw = P.dma("gp", xn_scr[ts, :, :, :], xnb3[:, :, :], deps=[xnc], label="xnw")
            xnws[ts] = xnw
            stl = None
            for dj in range(8):
                bnk = dj % 2
                d0 = [selv[(ts, 7)], wcast[7], ("war", pbuse[bnk])]
                mmp = []
                for kc in range(8):
                    mmp.append(P.op("te", lambda e, dj=dj, kc=kc, bnk=bnk: e.matmul(
                        pb[bnk][:, :], wo_bf[:, dj, kc, :], hs_fin[:, kc, :],
                        start=(kc == 0), stop=(kc == 7)), deps=d0 if kc == 0 else [], label="pjmm"))
                stl = P.op("ve", lambda e, dj=dj, bnk=bnk: e.scalar_tensor_tensor(
                    stbf[:, dj, :], pb[bnk][:, :], bo[:, dj:dj + 1], xnf[:, dj, :],
                    op0=ALU.add, op1=ALU.add),
                    deps=[mmp[7], xnl, smalls[-1], ("war", cb2)], label="stev")
                pbuse[bnk] = stl
            stw = P.dma("gp", st_scr[ts, :, :, :], stbf[:, :, :], deps=[stl], label="stw")
            stws[ts] = stw
            cwevs = []
            prevtrp = None
            prevcw2 = None
            for grp in range(4):
                d0 = [xnc, c_wr, smalls[-1], smalls[-1], ("war", pbuse[2])]
                sd = P.op("te", lambda e: e.matmul(pb[2][0:128, 0:8], onr_f[:, :],
                                                   brr_f[0:1, :], start=True, stop=False),
                          deps=d0, label="rsd")
                mmr = None
                for dc in range(8):
                    mmr = P.op("te", lambda e, grp=grp, dc=dc: e.matmul(
                        pb[2][0:128, 0:8], xnb3[:, dc, grp * 128:(grp + 1) * 128],
                        wr_b[:, dc * 8:(dc + 1) * 8], start=False, stop=(dc == 7)),
                        deps=[], label="rmm")
                v1_ = P.op("ve", lambda e: e.tensor_reduce(v1sb[:, :], pb[2][0:128, 0:8],
                                                           axis=mybir.AxisListType.X, op=ALU.max),
                           deps=[mmr], label="rv1")
                m1_ = P.op("ve", lambda e: e.tensor_scalar(m1sb[:, :], pb[2][0:128, 0:8], v1sb[:, 0:1],
                                                           None, op0=ALU.is_ge), deps=[v1_], label="rm1")
                l2_ = P.op("ve", lambda e: e.scalar_tensor_tensor(l2sb[:, :], m1sb[:, :], -1e30,
                                                                  pb[2][0:128, 0:8], op0=ALU.mult,
                                                                  op1=ALU.add), deps=[m1_], label="rl2")
                pbuse[2] = l2_
                v2_ = P.op("ve", lambda e: e.tensor_reduce(v2sb[:, :], l2sb[:, :],
                                                           axis=mybir.AxisListType.X, op=ALU.max),
                           deps=[l2_], label="rv2")
                m2_ = P.op("ve", lambda e: e.tensor_scalar(m2sb[:, :], l2sb[:, :], v2sb[:, 0:1],
                                                           None, op0=ALU.is_ge), deps=[v2_], label="rm2")
                w1_ = P.op("se", lambda e: e.activation(w1w[:, :], v2sb[:, :], AF.Sigmoid,
                                                        scale=-1.0, bias=v1sb[:, 0:1]),
                           deps=[v2_, v1_, ("war", prevcw2)], label="rw1")
                w2_ = P.op("ve", lambda e: e.tensor_scalar(w2w[:, :], w1w[:, :], -1.0, 1.0,
                                                           op0=ALU.mult, op1=ALU.add),
                           deps=[w1_], label="rw2")
                cw1 = P.op("ve", lambda e: e.tensor_scalar(cwsb[:, :], m1sb[:, :], w1w[:, 0:1],
                                                           None, op0=ALU.mult),
                           deps=[w1_, m1_, ("war", prevtrp)], label="rcw1")
                cw2 = P.op("ve", lambda e: e.scalar_tensor_tensor(cwsb[:, :], m2sb[:, :], w2w[:, 0:1],
                                                                  cwsb[:, :], op0=ALU.mult,
                                                                  op1=ALU.add),
                           deps=[w2_, m2_, cw1], label="rcw2")
                prevcw2 = cw2
                trp = P.op("te", lambda e: e.transpose(pb[3][0:8, 0:128], cwsb[:, :], idf[:, :]),
                           deps=[cw2, smalls[-1], ("war", pbuse[3])], label="rtr")
                prevtrp = trp
                cwe = P.op("se", lambda e, grp=grp: e.activation(cwt8f[0:8, grp * 128:(grp + 1) * 128],
                                                                 pb[3][0:8, 0:128], AF.Identity),
                           deps=[trp, ("war", cwws[ts - 1] if (ts and grp == 0) else None)],
                           label="cwev")
                pbuse[3] = cwe
                cwevs.append(cwe)
            cww = P.dma("gp", cw_scr[ts, :, :], cwt8f[0:16, :], deps=cwevs, label="cww")
            cwws[ts] = cww
            pproj[ts] = dict(stl=stl, stw=stw, xnc=xnc, xnl=xnl, mmS2_7=mmS2b[7],
                             rmmlast=mmr)

        mlast = {}
        outws = []
        prev_final = {}
        for half in range(2):
            azdeps = []
            if half == 0:
                azdeps = [stws[NTS - 1], ("war", selv[(NTS - 1, 7)])] + [cwws[q] for q in range(NTS)]
            else:
                azdeps = [("war", prev_final["acclast"])]
                if debug_scratch:
                    azdeps.append(("war", prev_final["accdbg"]))
            az = P.op("ve", lambda e: e.memset(accb.rearrange("p a b -> p (a b)"), 0.0),
                      deps=azdeps, label="acz")
            lh1 = {0: None, 1: None}
            lym = {0: None, 1: None}
            w1cprev = {0: mlast.get((half - 1, "w1cp", 0)), 1: mlast.get((half - 1, "w1cp", 1))}
            w2cprev = {0: mlast.get((half - 1, "w2cp", 0)), 1: mlast.get((half - 1, "w2cp", 1))}
            lastacc = None
            first_guard = ([("war", stws[NTS - 1]), ("war", cwws[NTS - 1]),
                            ("war", selv[(NTS - 1, 7)]), ("war", pproj[NTS - 1]["xnl"]),
                            ("war", pproj[NTS - 1]["mmS2_7"])] if half == 0 else
                           [("war", prev_final.get("ollast"))])
            nw = {"n": 0}
            for e_ in range(E):
                for tloc in range(4):
                    ts = half * 4 + tloc
                    xnwar = mlast.get((half, "xn_rd"))
                    xnld = P.dma("sy", xnb3m[:, :, :], xn_scr[ts, :, :, :],
                                 deps=[xnws[NTS - 1], ("war", xnwar)] +
                                 ([("war", cwws[NTS - 1])] if (half == 0 and e_ == 0 and tloc == 0) else []) +
                                 ([("war", prev_final.get("o1last"))] if (half == 1 and e_ == 0 and tloc == 0) else []),
                                 label="xnld")
                    cwld = P.dma("sy", cwrow[0:1, :], cw_scr[ts, e_:e_ + 1, :],
                                 deps=[cwws[NTS - 1], ("war", mlast.get((half, "cw_rd")))], label="cwld")
                    h1l = None
                    for fc in range(32):
                        par = fc % 2
                        stag = w1st0 if par == 0 else w1st1
                        g0 = first_guard if nw["n"] < 2 else []
                        ldw1 = P.dma("sy", stag[:, :, :], w1b[e_, fc, :, :, :],
                                     deps=[("war", w1cprev[par])] + g0, label=f"w1ld{par}")
                        cst1 = P.op("ve", lambda e, par=par, stag=stag: e.tensor_copy(
                            w1bb[:, par, :, :].rearrange("p a b -> p (a b)"),
                            stag.rearrange("p a b -> p (a b)")),
                            deps=[ldw1, ("war", lh1[par])] + g0, label="w1c")
                        w1cprev[par] = cst1
                        nw["n"] += 1
                        bnk = fc % 2
                        d0 = [xnld, cst1, ("war", pbuse[bnk])]
                        mmh = []
                        for dc in range(8):
                            mmh.append(P.op("te", lambda e, fc=fc, dc=dc, bnk=bnk: e.matmul(
                                pb[bnk][:, :], w1bb[:, fc % 2, dc, :], xnb3m[:, dc, :],
                                start=(dc == 0), stop=(dc == 7)), deps=d0 if dc == 0 else [],
                                label=("h1mmz" if dc == 7 else "h1mm")))
                        lh1[fc % 2] = mmh[7]
                        h1l = P.op("se", lambda e, fc=fc, e_=e_, bnk=bnk: e.activation(
                            h1big[:, fc, :], pb[bnk][:, :], AF.Gelu, bias=b1c[:, e_, fc:fc + 1]),
                            deps=[mmh[7], smalls[-1],
                                  ("war", mlast.get("h1dbg") if (debug_scratch and fc == 0) else None)],
                            label="h1ev")
                        pbuse[bnk] = h1l
                    if debug_scratch and half == 0 and e_ == 0 and tloc == 0:
                        mlast["h1dbg"] = P.dma("gp", h1_dbg[:, :, :], h1big[:, :, :],
                                               deps=[h1l], label="h1dbg")
                    mlast[(half, "xn_rd")] = lh1[1]
                    cwb_ = P.op("te", lambda e: e.matmul(pb[4][:, :], onr_f[:, :], cwrow[0:1, :],
                                                         start=True, stop=True),
                                deps=[cwld, smalls[-1], ("war", pbuse[4])], label="cwbc")
                    mlast[(half, "cw_rd")] = cwb_
                    cwbe = P.op("se", lambda e: e.activation(cwbf[:], pb[4][:, :], AF.Identity),
                                deps=[cwb_, ("war", mlast.get((half, "cwbf_rd"))),
                                      ("war", mlast.get("cwbdbg"))], label="cwbe")
                    pbuse[4] = cwbe
                    if debug_scratch and half == 0 and e_ == 0 and tloc == 0:
                        mlast["cwbdbg"] = P.dma("gp", cwb_dbg[:, :], cwbf[:], deps=[cwbe],
                                                label="cwbdbg")
                    lastyv = None
                    for dj in range(8):
                        par = dj % 2
                        stag = w2st0 if par == 0 else w2st1
                        g0 = first_guard if (half == 1 and e_ == 0 and tloc == 0 and dj < 2) else []
                        ldw2 = P.dma("sy", stag[:, :, :], w2b[e_, dj, :, :, :],
                                     deps=[("war", w2cprev[par])] + g0, label=f"w2ld{par}")
                        cst2 = P.op("ve", lambda e, par=par, stag=stag: e.tensor_copy(
                            w2bb[:, par, :, :].rearrange("p a b -> p (a b)"),
                            stag.rearrange("p a b -> p (a b)")),
                            deps=[ldw2, ("war", lym[par])] + g0, label="w2c")
                        w2cprev[par] = cst2
                        bnk = 2 + dj % 2
                        d0 = [h1l, cst2, ("war", pbuse[bnk])]
                        mmy = []
                        for fc in range(32):
                            mmy.append(P.op("te", lambda e, dj=dj, fc=fc, bnk=bnk: e.matmul(
                                pb[bnk][:, :], w2bb[:, dj % 2, fc, :], h1big[:, fc, :],
                                start=(fc == 0), stop=(fc == 31)), deps=d0 if fc == 0 else [],
                                label=("ymmz" if fc == 31 else "ymm")))
                        lym[dj % 2] = mmy[31]
                        yv = P.op("ve", lambda e, dj=dj, e_=e_, bnk=bnk: e.scalar_tensor_tensor(
                            yevf[:], pb[bnk][:, :], b2c[:, e_, dj:dj + 1], cwbf[:],
                            op0=ALU.add, op1=ALU.mult),
                            deps=[mmy[31], cwbe, smalls[-1]], label="yev")
                        pbuse[bnk] = yv
                        ac = P.op("ve", lambda e, dj=dj, tloc=tloc: e.tensor_tensor(
                            accb[:, dj, tloc * TT:(tloc + 1) * TT],
                            accb[:, dj, tloc * TT:(tloc + 1) * TT], yevf[:], op=ALU.add),
                            deps=[yv, az], label="acc")
                        lastyv = ac
                        mlast[(half, e_, tloc, dj)] = ac
                    mlast[(half, "cwbf_rd")] = lastyv
                    lastacc = lastyv

            accdbg_op = None
            mlast[(half, "w1cp", 0)] = w1cprev[0]
            mlast[(half, "w1cp", 1)] = w1cprev[1]
            mlast[(half, "w2cp", 0)] = w2cprev[0]
            mlast[(half, "w2cp", 1)] = w2cprev[1]
            if debug_scratch:
                accdeps_all = [mlast[(half, E - 1, tl_, dj)] for tl_ in range(4) for dj in range(8)]
                accdbg_op = P.dma("gp", acc_dbg[half, :, :, :], accb[:, :, :],
                                  deps=accdeps_all, label="accdbg")
            pf = {}
            for tloc in range(4):
                ts = half * 4 + tloc
                accdeps = [mlast[(half, E - 1, tloc, dj)] for dj in range(8)]
                olwar = ("war", pf.get("ol"))
                mwar = [("war", lh1[0]), ("war", lh1[1]), ("war", lym[0]), ("war", lym[1])] \
                    if tloc == 0 else []
                stld = P.dma("sy", stbf2[:, :, :], st_scr[ts, :, :, :],
                             deps=[stws[NTS - 1], ("war", mlast[(half, "xn_rd")] if tloc == 0 else None),
                                   ("war", pf.get("o1last"))], label="stld")
                xld2 = P.dma("sy", xsf[:, :, :], xTs[:, :, ts * TT:(ts + 1) * TT],
                             deps=[("war", w2cprev[0] if tloc == 0 else None),
                                   ("war", w2cprev[1] if tloc == 0 else None),
                                   ("war", pf.get("ol"))] +
                             ([("war", outws[-1])] if outws else []), label="xld2")
                xnld2 = P.dma("sy", xnbf2[:, :, :], xn_scr[ts, :, :, :],
                              deps=[xnws[NTS - 1], ("war", pf.get("zc"))], label="xnld2")
                zc = P.op("ve", lambda e, tloc=tloc: e.tensor_tensor(
                    zb[:, :, :],
                    accb[:, :, tloc * TT:(tloc + 1) * TT],
                    xnbf2[:, :, :], op=ALU.add),
                    deps=[xnld2] + accdeps + mwar + [("war", pf.get("fz1last"))], label="zc")
                sqz = P.op("se", lambda e: e.activation(sqzb.rearrange("p c d -> p (c d)"),
                                                        zb.rearrange("p c d -> p (c d)"), AF.Square),
                           deps=[zc, ("war", pf.get("mmZ2_7"))] + mwar, label="sqz")
                mmZ1 = []
                for dc in range(8):
                    d0 = [zc, ("war", pbuse[4])] if dc == 0 else []
                    mmZ1.append(P.op("te", lambda e, dc=dc: e.matmul(pb[4][0:1, :], onc_b[:, :],
                                                                     zb[:, dc, :], start=(dc == 0),
                                                                     stop=(dc == 7)), deps=d0,
                                     label="mmZ1"))
                mmZ2 = []
                for dc in range(8):
                    d0 = [sqz, ("war", pbuse[5])] if dc == 0 else []
                    mmZ2.append(P.op("te", lambda e, dc=dc: e.matmul(pb[5][0:1, :], onc_b[:, :],
                                                                     sqzb[:, dc, :], start=(dc == 0),
                                                                     stop=(dc == 7)), deps=d0,
                                     label="mmZ2"))
                mrz = P.op("se", lambda e: e.activation(rowa[0:1, 0, :], pb[4][0:1, :], AF.Identity,
                                                        scale=1.0 / D),
                           deps=[mmZ1[7], ("war", pf.get("Bz"))], label="mrow")
                msz = P.op("se", lambda e: e.activation(rowa[0:1, 1, :], pb[4][0:1, :], AF.Square,
                                                        scale=1.0 / D), deps=[mmZ1[7]], label="msq")
                pbuse[4] = msz
                vrz = P.op("ve", lambda e: e.scalar_tensor_tensor(rowa[0:1, 2, :], pb[5][0:1, :],
                                                                  1.0 / D, rowa[0:1, 1, :],
                                                                  op0=ALU.mult, op1=ALU.subtract),
                           deps=[mmZ2[7], msz, ("war", pf.get("rz"))], label="var")
                pbuse[5] = vrz
                sqz2 = P.op("se", lambda e: e.activation(rowa[0:1, 3, :], rowa[0:1, 2, :], AF.Sqrt),
                            deps=[vrz, ("war", pf.get("rz"))], label="sqz2")
                rz = P.op("ve", lambda e: e.reciprocal(rowa[0:1, 5, :], rowa[0:1, 3, :]),
                          deps=[sqz2, ("war", pf.get("Bz"))], label="rz")
                Bz = P.op("ve", lambda e: e.tensor_tensor(rowa[0:1, 6, :], rowa[0:1, 0, :],
                                                          rowa[0:1, 5, :], op=ALU.mult),
                          deps=[rz, mrz], label="Bz")
                bcAz = P.op("te", lambda e: e.matmul(pb[6][:, :], onr_f[:, :], rowa[0:1, 5, :],
                                                     start=True, stop=True),
                            deps=[rz, smalls[-1], ("war", pbuse[6])], label="bcAz")
                bcBz = P.op("te", lambda e: e.matmul(pb[7][:, :], onr_f[:, :], rowa[0:1, 6, :],
                                                     start=True, stop=True),
                            deps=[Bz, smalls[-1], ("war", pbuse[7])], label="bcBz")
                ol = None
                o1last = None
                fz1last = None
                for dc in range(8):
                    tz1 = P.op("ve", lambda e, dc=dc: e.tensor_tensor(tg1[:], zb[:, dc, :],
                                                                      pb[6][:, :], op=ALU.mult),
                               deps=[bcAz] if dc == 0 else [], label="fz1")
                    fz1last = tz1
                    tz2 = P.op("ve", lambda e: e.tensor_tensor(tg2[:], tg1[:], pb[7][:, :],
                                                               op=ALU.subtract),
                               deps=[tz1] + ([bcBz] if dc == 0 else []), label="fz2")
                    tz3 = P.op("ve", lambda e, dc=dc: e.tensor_scalar(tg1[:], tg2[:],
                                                                      lw2[:, dc:dc + 1],
                                                                      lb2[:, dc:dc + 1],
                                                                      op0=ALU.mult, op1=ALU.add),
                               deps=[tz2, smalls[-1], smalls[-1]], label="fz3")
                    sil = P.op("se", lambda e: e.activation(gsil[:], tg1[:], AF.Silu),
                               deps=[tz3, ("war", mlast.get((half, "cwbf_rd")) if (tloc == 0 and dc == 0) else None)],
                               label="sil")
                    o1 = P.op("ve", lambda e, dc=dc: e.tensor_tensor(o1f[:, dc, :], stbf2[:, dc, :],
                                                                     gsil[:], op=ALU.mult),
                              deps=[sil, stld] + ([("war", outws[-1])] if (outws and dc == 0) else []),
                              label="fo1")
                    o1last = o1
                    ol = P.op("ve", lambda e, dc=dc: e.tensor_tensor(o1f[:, dc, :], o1f[:, dc, :],
                                                                     xsf[:, dc, :], op=ALU.add),
                              deps=[o1, xld2], label="fo2")
                pbuse[6] = ol
                pbuse[7] = ol
                ow = P.dma("gp", outT[ts, :, :, :], o1f[:, :, :], deps=[ol], label="outw")
                outws.append(ow)
                pf = dict(ol=ol, o1last=o1last, zc=zc, mmZ2_7=mmZ2[7], Bz=Bz, rz=rz,
                          fz1last=fz1last)
            prev_final = dict(acclast=pf["zc"], ollast=pf["ol"], accdbg=accdbg_op,
                              o1last=pf["o1last"])
        finals = outws

    P.op("gp", lambda e: e.nop(), deps=list(finals), label="fin")

    block = es.enter_context(nc.Block())
    P.finalize(block, es)
    es.close()
    return nc


def host_inputs(inp, core):
    x = np.asarray(inp["L0_attn_x"])
    x_sb = np.ascontiguousarray(x.transpose(1, 0, 2)).reshape(T, D)
    xTf = np.ascontiguousarray(np.ascontiguousarray(x_sb.T).reshape(8, 128, T).transpose(1, 0, 2))
    sh = x_sb[core * TSH:(core + 1) * TSH]
    xTs_ = np.ascontiguousarray(np.ascontiguousarray(sh.T).reshape(8, 128, TSH).transpose(1, 0, 2))

    def blkT(Wt, no, ni):
        return np.ascontiguousarray(Wt.reshape(no, 128, ni, 128).transpose(1, 0, 2, 3))

    def blkT_mk(Wt, no, ni):
        return np.ascontiguousarray(Wt.reshape(no, 128, ni, 128).transpose(1, 2, 0, 3))

    w_in_b = blkT(np.ascontiguousarray(np.asarray(inp["L0_attn_W_in"]).T), 8, 16).reshape(128, 16384)
    w_h_b = blkT_mk(np.ascontiguousarray(np.asarray(inp["L0_attn_W_h"]).T), 8, 8).reshape(128, 8192)
    w_out_b = blkT_mk(np.ascontiguousarray(np.asarray(inp["L0_attn_W_out"]).T), 8, 8).reshape(128, 8192)
    W1 = np.asarray(inp["L0_ffn_W1"])
    W2 = np.asarray(inp["L0_ffn_W2"])
    w1b_ = np.empty((E, 32, 128, 8, 128), np.float32)
    w2b_ = np.empty((E, 8, 128, 32, 128), np.float32)
    for e in range(E):
        w1t = np.ascontiguousarray(W1[e].T)
        w1b_[e] = w1t.reshape(8, 128, 32, 128).transpose(2, 1, 0, 3)
        w2t = np.ascontiguousarray(W2[e].T)
        w2b_[e] = w2t.reshape(32, 128, 8, 128).transpose(2, 1, 0, 3)
    Wr = np.asarray(inp["L0_ffn_Wr"])
    wrb_ = np.ascontiguousarray(np.ascontiguousarray(Wr.T).reshape(8, 128, 8).transpose(1, 0, 2)).reshape(128, 64)

    def cols(v):
        return np.ascontiguousarray(np.asarray(v, dtype=np.float32).reshape(8, 128).T)

    b_in = np.asarray(inp["L0_attn_b_in"])
    flags_ = np.zeros((128, 8), np.float32)
    flags_[:, core] = 1.0
    return {
        "xT": xTf, "xTs": xTs_, "w_in": w_in_b, "w_h": w_h_b, "w_out": w_out_b,
        "w1b": w1b_, "w2b": w2b_, "wrb": wrb_,
        "bu_c": cols(b_in[:D]),
        "bv_c": cols(b_in[D:] + np.asarray(inp["L0_attn_b_h"])),
        "bo_c": cols(inp["L0_attn_b_out"]),
        "b1_c": np.ascontiguousarray(np.asarray(inp["L0_ffn_b1"]).reshape(E, 32, 128).transpose(2, 0, 1)),
        "b2_c": np.ascontiguousarray(np.asarray(inp["L0_ffn_b2"]).reshape(E, 8, 128).transpose(2, 0, 1)),
        "lnw_c": cols(inp["L0_attn_ln_w"]), "lnb_c": cols(inp["L0_attn_ln_b"]),
        "lw2_c": cols(inp["L0_ffn_ln_w"]), "lb2_c": cols(inp["L0_ffn_ln_b"]),
        "br_r": np.broadcast_to(np.asarray(inp["L0_ffn_br"], dtype=np.float32).reshape(1, 8), (16, 8)).copy(),
        "id128": np.eye(128, dtype=np.float32),
        "ones_c": np.ones((128, 1), np.float32),
        "ones_r": np.ones((1, 128), np.float32),
        "flags": flags_,
    }


def kernel(**inputs):
    nc = build(upto=3)
    in_maps = [host_inputs(inputs, c) for c in range(8)]
    res = run_bass_kernel_spmd(nc, in_maps, core_ids=list(range(8)))
    outs = []
    for c in range(8):
        o = np.asarray(res.results[c]["outT"])
        o = o.transpose(0, 3, 2, 1).reshape(TSH, D)
        outs.append(o)
    full = np.concatenate(outs, axis=0)
    return np.ascontiguousarray(full.reshape(S, 4, D).transpose(1, 0, 2))
